# version 23
# baseline (speedup 1.0000x reference)
import contextlib
import numpy as np
import concourse.bass as bass
import concourse.mybir as mybir
from concourse.bass_utils import run_bass_kernel_spmd

F32 = mybir.dt.float32
BF16 = mybir.dt.bfloat16
AF = mybir.ActivationFunctionType
ALU = mybir.AluOpType

B, L, D, NM = 8, 4096, 1024, 16
SEG = 1024
NSEG = L // SEG
TT = SEG // 128
DC = D // 128
NE = 10
NTB = SEG // 512
ROUTED_SCALE = 2.5


class Sem:
    def __init__(self, h):
        self.h = h
        self.v = 0

    def inc(self, n=1):
        self.v += n
        return self.v


def build_nc(debug=False, ablate=None):
    nc = bass.Bass("TRN2", target_bir_lowering=False)

    x_h = nc.dram_tensor("x", [L, D], F32, kind="ExternalInput").ap()
    y_h = nc.dram_tensor("y", [L, D], F32, kind="ExternalOutput").ap()
    if debug:
        dq_h = nc.dram_tensor("dq", [128, 128], F32, kind="ExternalOutput").ap()
        dc_h = nc.dram_tensor("dc", [128, 128], F32, kind="ExternalOutput").ap()
        dst_h = nc.dram_tensor("dst", [128, 128], F32, kind="ExternalOutput").ap()
        dwtm_h = nc.dram_tensor("dwtm", [128, 64], F32, kind="ExternalOutput").ap()
        dsfc_h = nc.dram_tensor("dsfc", [128, 64], F32, kind="ExternalOutput").ap()
        dxn_h = nc.dram_tensor("dxn", [128, 8192], mybir.dt.bfloat16, kind="ExternalOutput").ap()
        dssm_h = nc.dram_tensor("dssm", [128, 1024], mybir.dt.bfloat16, kind="ExternalOutput").ap()
        dxs_h = nc.dram_tensor("dxs", [128, 1024], F32, kind="ExternalOutput").ap()
        dxnst_h = nc.dram_tensor("dxnst", [128, 1024], mybir.dt.bfloat16, kind="ExternalOutput").ap()
        dbn_h = nc.dram_tensor("dbn", [128, 24], F32, kind="ExternalOutput").ap()
    dl_h = nc.dram_tensor("delta_p", [128, 128], F32, kind="ExternalInput").ap()
    al_h = nc.dram_tensor("alpha_p", [128, 128], F32, kind="ExternalInput").ap()
    bt_h = nc.dram_tensor("beta_p", [128, 128], F32, kind="ExternalInput").ap()
    gm_h = nc.dram_tensor("gamma_p", [128, 128], F32, kind="ExternalInput").ap()
    om_h = nc.dram_tensor("omega_p", [128, 8], F32, kind="ExternalInput").ap()
    gw_h = nc.dram_tensor("gw_p", [128, 64], F32, kind="ExternalInput").ap()
    gb_h = nc.dram_tensor("gb_p", [8, 1], F32, kind="ExternalInput").ap()
    wg_h = nc.dram_tensor("w_gate", [8, D, 512], F32, kind="ExternalInput").ap()
    wu_h = nc.dram_tensor("w_up", [8, D, 512], F32, kind="ExternalInput").ap()
    wd_h = nc.dram_tensor("w_down", [8, 512, D], F32, kind="ExternalInput").ap()
    swg_h = nc.dram_tensor("sw_gate", [D, D], F32, kind="ExternalInput").ap()
    swu_h = nc.dram_tensor("sw_up", [D, D], F32, kind="ExternalInput").ap()
    swd_h = nc.dram_tensor("sw_down", [D, D], F32, kind="ExternalInput").ap()

    ctx = contextlib.ExitStack()
    E = ctx.enter_context

    y_acc = E(nc.sbuf_tensor("y_acc", [128, TT * 1024], F32))
    xn_fm = [E(nc.sbuf_tensor(f"xn_fm{i}", [128, DC * SEG], BF16)) for i in range(2)]
    wg_sb = [E(nc.sbuf_tensor(f"wg{i}", [128, 8 * 512], BF16)) for i in range(2)]
    wu_sb = [E(nc.sbuf_tensor(f"wu{i}", [128, 8 * 512], BF16)) for i in range(2)]
    wd_sb = [E(nc.sbuf_tensor(f"wd{i}", [128, 4 * 1024], BF16)) for i in range(2)]
    s_all = E(nc.sbuf_tensor("s_all", [128, 12 * 512], BF16))
    sacc_bf = [E(nc.sbuf_tensor(f"saccb{i}", [128, SEG], BF16)) for i in range(2)]
    ssm_tm = [E(nc.sbuf_tensor(f"ssmtm{i}", [128, SEG], BF16)) for i in range(2)]
    diag_all = [E(nc.sbuf_tensor(f"diag_all{i}", [128, 32 * 128], BF16)) for i in range(2)]
    ko_diag = [E(nc.sbuf_tensor(f"kodg{i}", [128, 128], BF16)) for i in range(2)]
    u_buf = [E(nc.sbuf_tensor(f"ubuf{i}", [128, 512], F32)) for i in range(2)]
    q2_all = E(nc.sbuf_tensor("q2_all", [128, 128], F32))
    cq_all = E(nc.sbuf_tensor("cq_all", [128, 128], F32))
    kom = E(nc.sbuf_tensor("kom", [128, 8], F32))
    corr8 = E(nc.sbuf_tensor("corr8", [128, 8], F32))
    cscr = E(nc.sbuf_tensor("cscr", [128, 128], F32))
    om_diag = [E(nc.sbuf_tensor(f"omdg{i}", [128, 128], BF16)) for i in range(2)]
    hm = [E(nc.sbuf_tensor(f"hm{i}", [128, 4 * SEG], BF16)) for i in range(2)]
    g_sb = [E(nc.sbuf_tensor(f"gsb{i}", [128, 512], BF16)) for i in range(2)]
    x_stage = [E(nc.sbuf_tensor(f"xstg{i}", [128, 1024], F32)) for i in range(4)]
    xn_stage = [E(nc.sbuf_tensor(f"xnstg{i}", [128, 1024], BF16)) for i in range(4)]
    bn_s = E(nc.sbuf_tensor("bn_s", [128, 24], F32))
    p_t = E(nc.sbuf_tensor("p_t", [128, 128], F32))
    a_t = E(nc.sbuf_tensor("a_t", [128, 128], F32))
    q_all = E(nc.sbuf_tensor("q_all", [128, 128], F32))
    c_all = E(nc.sbuf_tensor("c_all", [128, 128], F32))
    bt_t = E(nc.sbuf_tensor("bt_t", [128, 128], F32))
    gm_t = E(nc.sbuf_tensor("gm_t", [128, 128], F32))
    om_t = E(nc.sbuf_tensor("om_t", [128, 8], F32))
    gwt = E(nc.sbuf_tensor("gwt", [128, 64], F32))
    gwt_bf = E(nc.sbuf_tensor("gwt_bf", [128, 64], BF16))
    gbt = E(nc.sbuf_tensor("gbt", [8, 1], F32))
    states = E(nc.sbuf_tensor("states", [128, 128], F32))
    id_f32 = E(nc.sbuf_tensor("id_f32", [128, 128], F32))
    id_int = E(nc.sbuf_tensor("id_int", [128, 128], mybir.dt.int32))
    sfc_fm = E(nc.sbuf_tensor("sfc_fm", [8, SEG], F32))
    sfc_tm = E(nc.sbuf_tensor("sfc_tm", [128, TT * 8], F32))
    gs_tm = E(nc.sbuf_tensor("gs_tm", [128, TT * 4], F32))
    rnk = E(nc.sbuf_tensor("rnk", [128, TT * 8], F32))
    selg = E(nc.sbuf_tensor("selg", [128, TT * 4], F32))
    tmp_e = E(nc.sbuf_tensor("tmp_e", [128, TT * 8], F32))
    w_tm = E(nc.sbuf_tensor("w_tm", [128, TT * 8], F32))
    scr = E(nc.sbuf_tensor("scr", [128, TT * 8], F32))
    gscr = E(nc.sbuf_tensor("gscr", [128, TT * 4], F32))
    den = E(nc.sbuf_tensor("den", [128, TT], F32))

    pg = [E(nc.psum_tensor(f"pg{i}", [128, 512], F32)) for i in range(2)]
    pu = [E(nc.psum_tensor(f"pu{i}", [128, 512], F32)) for i in range(2)]
    pw = E(nc.psum_tensor("pw", [128, 1024], F32))
    pm = E(nc.psum_tensor("pm", [128, 512], F32))
    pop1 = E(nc.psum_tensor("pop1", [128, 512], F32))

    sx = Sem(E(nc.semaphore("sx")))
    sxt = Sem(E(nc.semaphore("sxt")))
    sw = Sem(E(nc.semaphore("sw")))
    sst = Sem(E(nc.semaphore("sst")))
    sv = Sem(E(nc.semaphore("sv")))
    ss = Sem(E(nc.semaphore("ss")))
    st_ = Sem(E(nc.semaphore("st")))
    sg = Sem(E(nc.semaphore("sg")))

    steps = {"sync": [], "gpsimd": [], "vector": [], "scalar": [], "tensor": []}

    def emit(eng, fn):
        steps[eng].append(fn)

    def wait(eng, sem, val):
        if val > 0:
            emit(eng, lambda e, s=sem, v=val: e.wait_ge(s.h, v))

    def run(eng, fn, sem=None, n=1):
        if sem is None:
            emit(eng, lambda e, fn=fn: fn(e))
            return None
        v = sem.inc(n)
        emit(eng, lambda e, fn=fn, s=sem, n=n: fn(e).then_inc(s.h, n))
        return v

    run("gpsimd", lambda e: e.iota(out=id_int[:], pattern=[[1, 128]], base=0,
                                   channel_multiplier=-1))
    run("gpsimd", lambda e: e.tensor_scalar(out=id_int[:], in0=id_int[:],
                                            scalar1=0, scalar2=None,
                                            op0=ALU.is_equal))
    v_id = run("gpsimd", lambda e: e.tensor_copy(out=id_f32[:], in_=id_int[:]), sg)

    for dst, src in [(p_t, dl_h), (a_t, al_h), (bt_t, bt_h), (gm_t, gm_h),
                     (om_t, om_h), (gwt, gw_h), (gbt, gb_h)]:
        run("sync", lambda e, d=dst, s=src: e.dma_start(out=d[:], in_=s), sx, 16)
    v_params = sx.v

    def weight_batch_gu(eidx, slot):
        for dc in range(8):
            if eidx < 8:
                src_g = wg_h[eidx, dc * 128:(dc + 1) * 128, :]
                src_u = wu_h[eidx, dc * 128:(dc + 1) * 128, :]
            else:
                c0 = (eidx - 8) * 512
                src_g = swg_h[dc * 128:(dc + 1) * 128, c0:c0 + 512]
                src_u = swu_h[dc * 128:(dc + 1) * 128, c0:c0 + 512]
            run("gpsimd", lambda e, d=wg_sb[slot][:, dc * 512:(dc + 1) * 512],
                s=src_g: e.dma_start(out=d, in_=s), sw, 16)
            run("gpsimd", lambda e, d=wu_sb[slot][:, dc * 512:(dc + 1) * 512],
                s=src_u: e.dma_start(out=d, in_=s), sw, 16)
        return sw.v

    def weight_batch_d(eidx, slot):
        for ic in range(4):
            if eidx < 8:
                src_d = wd_h[eidx, ic * 128:(ic + 1) * 128, :]
            else:
                r0 = (eidx - 8) * 512 + ic * 128
                src_d = swd_h[r0:r0 + 128, :]
            run("gpsimd", lambda e, d=wd_sb[slot][:, ic * 1024:(ic + 1) * 1024],
                s=src_d: e.dma_start(out=d, in_=s), sw, 16)
        return sw.v

    def next_exp(s_, e_):
        if e_ + 2 < NE:
            return (s_, e_ + 2), e_ + 2
        if s_ + 1 < NSEG:
            return (s_ + 1, e_ + 2 - NE), e_ + 2 - NE
        return None, None

    w_gu_ready = {}
    w_d_ready = {}
    w_gu_ready[(0, 0)] = weight_batch_gu(0, 0)
    w_d_ready[(0, 0)] = weight_batch_d(0, 0)
    w_gu_ready[(0, 1)] = weight_batch_gu(1, 1)
    w_d_ready[(0, 1)] = weight_batch_d(1, 1)

    wait("scalar", sx, v_params)
    run("scalar", lambda e: e.activation(out=p_t[:], in_=p_t[:], func=AF.Sigmoid))
    v_sig = run("scalar", lambda e: e.activation(out=a_t[:], in_=a_t[:],
                                                 func=AF.Sigmoid), ss)
    wait("vector", ss, v_sig)
    run("vector", lambda e: e.tensor_tensor(out=q_all[:], in0=p_t[:], in1=a_t[:],
                                            op=ALU.mult))
    run("vector", lambda e: e.tensor_tensor(out=c_all[:], in0=p_t[:], in1=bt_t[:],
                                            op=ALU.mult))
    run("vector", lambda e: e.tensor_scalar(out=q_all[:], in0=q_all[:],
                                            scalar1=-1.0, scalar2=1.0,
                                            op0=ALU.mult, op1=ALU.add))
    run("vector", lambda e: e.tensor_tensor(out=c_all[:], in0=c_all[:],
                                            in1=gm_t[:], op=ALU.mult))
    run("vector", lambda e: e.tensor_scalar(
        out=c_all[:], in0=c_all[:], scalar1=0.25, scalar2=None, op0=ALU.mult))
    run("vector", lambda e: e.tensor_tensor(out=q2_all[:], in0=q_all[:],
                                            in1=q_all[:], op=ALU.mult))
    for d2 in range(8):
        run("vector", lambda e, d=d2: e.reduce_sum(
            out=kom[:, d:d + 1], in_=c_all[:, d * 16:(d + 1) * 16],
            axis=mybir.AxisListType.X))
    run("vector", lambda e: e.tensor_tensor(out=cq_all[:], in0=c_all[:],
                                            in1=q_all[:], op=ALU.mult))
    run("vector", lambda e: e.tensor_copy(out=gwt_bf[:], in_=gwt[:]))
    v_ssmparams = run("vector", lambda e: e.tensor_tensor(
        out=kom[:], in0=kom[:], in1=om_t[:], op=ALU.add), sv)

    ln_done = {}
    xt_done = {}
    store_done = {0: 0}
    tr_fill_done = {}
    tr_add_done = {}
    hmult_done = {}
    yadd_done = {}
    pair_done = {}
    silu_done = {}
    down_done = {}
    outproj_done = {}
    evac_done = {}
    omdiag_done = {}
    sstm_done = {}
    sadd_done = {}
    diag_done = {}
    scan_done = {}
    seg_lastmm = {}
    gate_done = {}
    gate_evac = {}
    copy_chain = []
    scan_idx = [0]

    seg_xt_map = {}
    xreload_map = {}

    x_load_done = {}

    def emit_p1_load(s, tt):
        rows = s * SEG + tt * 128
        b = tt % 4
        if tt >= 4:
            wait("sync", sv, ln_done[(s, tt - 4)])
        x_load_done[(s, tt)] = run("sync", lambda e, r=rows, b=b: e.dma_start(
            out=x_stage[b][:], in_=x_h[r:r + 128, :]), sx, 16)

    p1_vs = {}

    def emit_p1_a(s, tt):
        slot = s % 2
        b = tt % 4
        if (s, tt) not in x_load_done:
            emit_p1_load(s, tt)
        wait("vector", sx, x_load_done[(s, tt)])
        if tt >= 4:
            wait("vector", sxt, xt_done[(s, tt - 4)])
        run("vector", lambda e, b=b: e.bn_stats(out=bn_s[:, 0:6],
                                                in_=x_stage[b][:, 0:512]))
        run("vector", lambda e, b=b: e.bn_stats(out=bn_s[:, 6:12],
                                                in_=x_stage[b][:, 512:1024]))
        def vtt(o, i0_, i1_, op):
            run("vector", lambda e, o=o, a=i0_, b2=i1_, op=op: e.tensor_tensor(
                out=bn_s[:, o:o + 1], in0=bn_s[:, a:a + 1],
                in1=bn_s[:, b2:b2 + 1], op=op))
        vtt(12, 1, 4, ALU.add)
        vtt(13, 7, 10, ALU.add)
        vtt(14, 2, 5, ALU.add)
        vtt(15, 8, 11, ALU.add)
        vtt(16, 1, 1, ALU.mult)
        vtt(17, 4, 4, ALU.mult)
        vtt(18, 7, 7, ALU.mult)
        vtt(19, 10, 10, ALU.mult)
        vtt(12, 12, 13, ALU.add)
        vtt(14, 14, 15, ALU.add)
        vtt(16, 16, 17, ALU.add)
        vtt(18, 18, 19, ALU.add)
        vtt(11, 11, 11, ALU.max)
        run("vector", lambda e: e.tensor_scalar(out=bn_s[:, 12:13],
            in0=bn_s[:, 12:13], scalar1=0.25, scalar2=None, op0=ALU.mult))
        vtt(16, 16, 18, ALU.add)
        vtt(11, 11, 11, ALU.max)
        vtt(20, 12, 12, ALU.mult)
        run("vector", lambda e: e.scalar_tensor_tensor(out=bn_s[:, 14:15],
            in0=bn_s[:, 16:17], scalar=256.0, in1=bn_s[:, 14:15],
            op0=ALU.mult, op1=ALU.add))
        vtt(13, 13, 13, ALU.add)
        vtt(11, 11, 11, ALU.max)
        run("vector", lambda e: e.scalar_tensor_tensor(out=bn_s[:, 21:22],
            in0=bn_s[:, 14:15], scalar=1.0 / 1024.0, in1=bn_s[:, 20:21],
            op0=ALU.mult, op1=ALU.subtract))
        vtt(15, 15, 15, ALU.add)
        vtt(11, 11, 11, ALU.max)
        vv = run("vector", lambda e: e.tensor_scalar(
            out=bn_s[:, 21:22], in0=bn_s[:, 21:22], scalar1=1e-5,
            scalar2=None, op0=ALU.add), sv)
        wait("scalar", sv, vv)
        vs = run("scalar", lambda e: e.activation(
            out=bn_s[:, 22:23], in_=bn_s[:, 21:22], func=AF.Sqrt), ss)
        p1_vs[(s, tt)] = vs

    def emit_p1_b(s, tt):
        slot = s % 2
        b = tt % 4
        vs = p1_vs[(s, tt)]
        wait("vector", ss, vs)
        vv = run("vector", lambda e: e.reciprocal(out=bn_s[:, 23:24],
                                                  in_=bn_s[:, 22:23]), sv)
        wait("vector", sv, vv)
        run("vector", lambda e: e.tensor_tensor(
            out=bn_s[:, 11:12], in0=bn_s[:, 11:12], in1=bn_s[:, 11:12],
            op=ALU.max))
        vln = run("vector", lambda e, b=b: e.tensor_scalar(
            out=xn_stage[b][:], in0=x_stage[b][:],
            scalar1=bn_s[:, 12:13], scalar2=bn_s[:, 23:24],
            op0=ALU.subtract, op1=ALU.mult), sv)
        ln_done[(s, tt)] = vln
        wait("sync", sv, vln)
        if s >= 2 and tt == 0:
            wait("sync", st_, seg_lastmm[s - 2])
        for dc in range(DC):
            run("sync", lambda e, d=dc, t=tt, b=b, sl=slot: e.dma_start_transpose(
                out=xn_fm[sl][:, d * SEG + t * 128:d * SEG + (t + 1) * 128],
                in_=xn_stage[b][:, d * 128:(d + 1) * 128]), sxt, 16)
        xt_done[(s, tt)] = sxt.v
        if tt == TT - 1:
            seg_xt_map[s] = sxt.v

    def emit_reloads(s):
        for tt in range(TT):
            rows = s * SEG + tt * 128
            wait("sync", sst, store_done[s])
            run("sync", lambda e, r=rows, t=tt: e.dma_start(
                out=y_acc[:, t * 1024:(t + 1) * 1024], in_=x_h[r:r + 128, :]),
                sx, 16)
        xreload_map[s] = sx.v

    p2_sig = {}


    def emit_p1_tile(s, tt):
        emit_p1_a(s, tt)
        emit_p1_b(s, tt)

    def emit_p2_logits(s):
        slot = s % 2
        wait("tensor", sx, v_params)
        if s == 0:
            wait("tensor", sg, v_id)
            wait("tensor", sv, v_ssmparams)
        else:
            wait("tensor", ss, evac_done[(s - 1, 7)])
        for tb in range(NTB):
            wait("tensor", sxt, xt_done[(s, tb * 4 + 3)])
            vt = None
            for dc in range(DC):
                sem = st_ if dc == 7 else None
                vt = run("tensor", lambda e, d=dc, t=tb, sl=slot: e.matmul(
                    out=pm[0:8, 0:512],
                    lhsT=gwt_bf[:, d * 8:(d + 1) * 8],
                    rhs=xn_fm[sl][:, d * SEG + t * 512:d * SEG + (t + 1) * 512],
                    start=(d == 0), stop=(d == 7)), sem) or vt
            wait("scalar", st_, vt)
            vs = run("scalar", lambda e, t=tb: e.activation(
                out=sfc_fm[0:8, t * 512:(t + 1) * 512], in_=pm[0:8, 0:512],
                func=AF.Sigmoid), ss)
            wait("tensor", ss, vs)
        p2_sig[s] = ss.v

    def emit_p2_rest(s):
        slot = s % 2
        wait("vector", ss, p2_sig[s])
        vv = run("vector", lambda e: e.tensor_scalar(
            out=sfc_fm[0:8, :], in0=sfc_fm[0:8, :], scalar1=gbt[0:8, 0:1],
            scalar2=None, op0=ALU.add), sv)
        wait("tensor", sv, vv)
        vt = None
        for tt in range(TT):
            sem = st_ if tt == TT - 1 else None
            vt = run("tensor", lambda e, t=tt: e.transpose(
                out=pm[:, 64 + t * 8:64 + (t + 1) * 8],
                in_=sfc_fm[0:8, t * 128:(t + 1) * 128],
                identity=id_f32[0:8, 0:8]), sem) or vt
        wait("scalar", st_, vt)
        ec1 = run("scalar", lambda e: e.copy(out=gbt[:], in_=gbt[:]), ss)
        ec2 = run("scalar", lambda e: e.copy(out=gbt[:], in_=gbt[:]), ss)
        wait("vector", st_, vt)
        wait("vector", ss, ec2)
        for _ in range(4):
            run("vector", lambda e: e.tensor_tensor(
                out=den[:], in0=den[:], in1=den[:], op=ALU.max))
        ge = run("vector", lambda e: e.tensor_copy(
            out=sfc_tm[:], in_=pm[:, 64:64 + TT * 8]), sv)
        gate_evac[s] = ge

        def ecol(tile, e_):
            return tile[:].rearrange("p (t e) -> p t e", e=8)[:, :, e_:e_ + 1]

        def gcol(tile, g_):
            return tile[:].rearrange("p (t g) -> p t g", g=4)[:, :, g_:g_ + 1]

        den3 = den[:].rearrange("p (t o) -> p t o", o=1)

        def filler(k=1):
            for _ in range(k):
                run("vector", lambda e: e.tensor_tensor(
                    out=ecol(scr, 0), in0=ecol(scr, 0), in1=ecol(scr, 0),
                    op=ALU.max))

        filler(2)
        for g in range(4):
            run("vector", lambda e, g=g: e.tensor_tensor(
                out=gcol(gs_tm, g), in0=ecol(sfc_tm, 2 * g),
                in1=ecol(sfc_tm, 2 * g + 1), op=ALU.add))
        gothers = {g: [gp for gp in range(4) if gp != g] for g in range(4)}

        def gcmp(dst, g, gp):
            op = ALU.is_ge if gp < g else ALU.is_gt
            run("vector", lambda e, d=dst, g=g, gp=gp, op=op: e.tensor_tensor(
                out=gcol(d, g), in0=gcol(gs_tm, gp), in1=gcol(gs_tm, g), op=op))

        for g in range(4):
            gcmp(selg, g, gothers[g][0])
        for g in range(4):
            gcmp(gscr, g, gothers[g][1])
        for g in range(4):
            run("vector", lambda e, g=g: e.tensor_tensor(
                out=gcol(selg, g), in0=gcol(selg, g), in1=gcol(gscr, g),
                op=ALU.add))
        for g in range(4):
            gcmp(gscr, g, gothers[g][2])
        for g in range(4):
            run("vector", lambda e, g=g: e.tensor_tensor(
                out=gcol(selg, g), in0=gcol(selg, g), in1=gcol(gscr, g),
                op=ALU.add))
        for g in range(4):
            run("vector", lambda e, g=g: e.tensor_scalar(
                out=gcol(selg, g), in0=gcol(selg, g), scalar1=2.0,
                scalar2=None, op0=ALU.is_lt))
        for e_ in range(8):
            run("vector", lambda e, e_=e_: e.tensor_tensor(
                out=ecol(tmp_e, e_), in0=ecol(sfc_tm, e_),
                in1=gcol(selg, e_ // 2), op=ALU.mult))
        eothers = {e_: [ep for ep in range(8) if ep != e_] for e_ in range(8)}

        def ecmp(dst, e_, ep):
            op = ALU.is_ge if ep < e_ else ALU.is_gt
            run("vector", lambda e, d=dst, e_=e_, ep=ep, op=op: e.tensor_tensor(
                out=ecol(d, e_), in0=ecol(tmp_e, ep), in1=ecol(tmp_e, e_), op=op))

        for e_ in range(8):
            ecmp(rnk, e_, eothers[e_][0])
        for r in range(1, 7):
            for e_ in range(8):
                ecmp(scr, e_, eothers[e_][r])
            for e_ in range(8):
                run("vector", lambda e, e_=e_: e.tensor_tensor(
                    out=ecol(rnk, e_), in0=ecol(rnk, e_), in1=ecol(scr, e_),
                    op=ALU.add))
        for e_ in range(8):
            run("vector", lambda e, e_=e_: e.tensor_scalar(
                out=ecol(rnk, e_), in0=ecol(rnk, e_), scalar1=2.0,
                scalar2=None, op0=ALU.is_lt))
        for e_ in range(8):
            run("vector", lambda e, e_=e_: e.tensor_tensor(
                out=ecol(w_tm, e_), in0=ecol(sfc_tm, e_),
                in1=ecol(rnk, e_), op=ALU.mult))
        for k in range(4):
            run("vector", lambda e, k=k: e.tensor_tensor(
                out=gcol(gscr, k), in0=ecol(w_tm, 2 * k),
                in1=ecol(w_tm, 2 * k + 1), op=ALU.add))
        run("vector", lambda e: e.tensor_tensor(
            out=gcol(selg, 0), in0=gcol(gscr, 0), in1=gcol(gscr, 1), op=ALU.add))
        filler(1)
        run("vector", lambda e: e.tensor_tensor(
            out=gcol(selg, 1), in0=gcol(gscr, 2), in1=gcol(gscr, 3), op=ALU.add))
        filler(2)
        run("vector", lambda e: e.tensor_tensor(
            out=den3, in0=gcol(selg, 0), in1=gcol(selg, 1), op=ALU.add))
        filler(2)
        run("vector", lambda e: e.tensor_scalar(
            out=den[:], in0=den[:], scalar1=1.0 / ROUTED_SCALE, scalar2=1e-20,
            op0=ALU.mult, op1=ALU.add))
        filler(2)
        run("vector", lambda e: e.reciprocal(out=den[:], in_=den[:]))
        filler(3)
        gd = None
        for e_ in range(8):
            sem = sv if e_ == 7 else None
            gd = run("vector", lambda e, e_=e_: e.tensor_tensor(
                out=ecol(w_tm, e_), in0=ecol(w_tm, e_),
                in1=den3, op=ALU.mult), sem) or gd
        gate_done[s] = gd

    def emit_rounds(s):
        slot = s % 2
        seg_xt = seg_xt_map[s]
        xreload = xreload_map[s]
        wait("vector", sxt, seg_xt)
        wait("vector", sx, xreload)
        wait("scalar", sx, v_params)
        if s == 0:
            wait("scalar", sg, v_id)
        run("scalar", lambda e: e.activation(
            out=om_diag[0][:], in_=id_f32[:], func=AF.Copy,
            scale=om_t[:, 0:1]), ss)
        od = run("scalar", lambda e: e.activation(
            out=ko_diag[0][:], in_=id_f32[:], func=AF.Copy,
            scale=kom[:, 0:1]), ss)
        omdiag_done[(s, 0)] = od

        def build_diags(dc2):
            dk2 = (s, dc2 - 2, 15) if dc2 >= 2 else (s - 1, dc2 + 6, 15)
            if dk2 in outproj_done:
                wait("scalar", st_, outproj_done[dk2])
            dv = None
            for n2 in range(NM):
                run("scalar", lambda e, n=n2, c=dc2 * 16 + n2, ab=dc2 % 2:
                    e.activation(
                    out=diag_all[ab][:, n * 128:(n + 1) * 128],
                    in_=id_f32[:], func=AF.Copy, scale=c_all[:, c:c + 1]))
                sem = ss if n2 == NM - 1 else None
                dv = run("scalar", lambda e, n=n2, c=dc2 * 16 + n2, ab=dc2 % 2:
                         e.activation(
                    out=diag_all[ab][:, (16 + n) * 128:(17 + n) * 128],
                    in_=id_f32[:], func=AF.Copy, scale=cq_all[:, c:c + 1]),
                    sem) or dv
            diag_done[(s, dc2)] = dv

        build_diags(0)
        if s > 0:
            wait("vector", ss, ss.v)
            run("vector", lambda e: e.tensor_tensor(
                out=cscr[:], in0=cq_all[:], in1=states[:], op=ALU.mult))
            run("vector", lambda e: e.tensor_tensor(
                out=bn_s[:, 11:12], in0=bn_s[:, 11:12], in1=bn_s[:, 11:12],
                op=ALU.max))
            run("vector", lambda e: e.tensor_tensor(
                out=bn_s[:, 11:12], in0=bn_s[:, 11:12], in1=bn_s[:, 11:12],
                op=ALU.max))
            for d2 in range(8):
                run("vector", lambda e, d=d2: e.reduce_sum(
                    out=corr8[:, d:d + 1], in_=cscr[:, d * 16:(d + 1) * 16],
                    axis=mybir.AxisListType.X))
        for r in range(11):
            dc = r if r < 8 else None
            if s + 1 < NSEG and r < 8 and (s + 1, r) not in x_load_done:
                emit_p1_load(s + 1, r)
            eg = r if r < 10 else None
            ed = r - 1 if r >= 1 else None
            dct = r - 2 if 2 <= r <= 9 else None
            if eg is not None:
                wait("tensor", sw, w_gu_ready[(s, eg)])
            if ed is not None:
                wait("tensor", sw, w_d_ready[(s, ed)])
            for k in range(16):
                if k == 5 and s + 1 < NSEG and 2 <= r <= 9:
                    emit_p1_b(s + 1, r - 2)
                if dc is not None and k == 0:
                    wait("tensor", ss, omdiag_done[(s, dc)])
                    pk = (s, dc - 1) if dc >= 1 else (s - 1, 7)
                    if pk in evac_done:
                        wait("tensor", ss, evac_done[pk])
                    if dc == 0:
                        wait("tensor", sv, gate_evac[s])
                    xodd = xn_fm[slot][:, dc * SEG:(dc + 1) * SEG].rearrange(
                        "p (m two) -> p m two", two=2)[:, :, 1:2]
                    xeven = xn_fm[slot][:, dc * SEG:(dc + 1) * SEG].rearrange(
                        "p (m two) -> p m two", two=2)[:, :, 0:1]
                    run("tensor", lambda e, d=dc, x=xodd: e.matmul(
                        out=pm[:], lhsT=om_diag[d % 2][:], rhs=x,
                        start=True, stop=False))
                    run("tensor", lambda e, d=dc, x=xeven: e.matmul(
                        out=pop1[:], lhsT=ko_diag[d % 2][:], rhs=x,
                        start=True, stop=False))
                if dc is not None and k in (5, 9, 13):
                    g = (k - 5) // 4
                    n_hi = 4 * g + 3
                    wait("tensor", sv, scan_done[(s, dc, n_hi)])
                    wait("tensor", ss, diag_done[(s, dc)])
                    for n in range(4 * g, 4 * g + 4):
                        run("tensor", lambda e, n=n, ab=dc % 2: e.matmul(
                            out=pm[:],
                            lhsT=diag_all[ab][:, n * 128:(n + 1) * 128],
                            rhs=s_all[:, (n % 12) * 512:(n % 12) * 512 + 512],
                            start=False, stop=False))
                        vt = run("tensor", lambda e, n=n, ab=dc % 2: e.matmul(
                            out=pop1[:, 1:512],
                            lhsT=diag_all[ab][:, (16 + n) * 128:(17 + n) * 128],
                            rhs=s_all[:, (n % 12) * 512:(n % 12) * 512 + 511],
                            start=False, stop=False), st_)
                        outproj_done[(s, dc, n)] = vt
                if eg is not None and k % 2 == 0:
                    j = k // 2
                    tb, ic = divmod(j, 4)
                    jb = j % 2
                    hk = (s, eg, j - 2) if j >= 2 else (s, eg - 1, 6 + j)
                    if hk in hmult_done:
                        wait("tensor", sv, hmult_done[hk])
                    vtg = None
                    for dcc in range(DC):
                        sem = st_ if dcc == 7 else None
                        vtg = run("tensor", lambda e, d=dcc, t=tb, i=ic, jb=jb,
                                  w=eg % 2, sl=slot: e.matmul(
                            out=pg[jb][:],
                            lhsT=wg_sb[w][:, d * 512 + i * 128:d * 512 + (i + 1) * 128],
                            rhs=xn_fm[sl][:, d * SEG + t * 512:d * SEG + (t + 1) * 512],
                            start=(d == 0), stop=(d == 7)), sem) or vtg
                    vtu = None
                    for dcc in range(DC):
                        sem = st_ if dcc == 7 else None
                        vtu = run("tensor", lambda e, d=dcc, t=tb, i=ic, jb=jb,
                                  w=eg % 2, sl=slot: e.matmul(
                            out=pu[jb][:],
                            lhsT=wu_sb[w][:, d * 512 + i * 128:d * 512 + (i + 1) * 128],
                            rhs=xn_fm[sl][:, d * SEG + t * 512:d * SEG + (t + 1) * 512],
                            start=(d == 0), stop=(d == 7)), sem) or vtu
                    pair_done[(s, eg, j)] = (vtg, vtu)
                if ed is not None and k % 2 == 1:
                    ttb = (k - 1) // 2
                    tb4 = ttb // 4
                    wait("tensor", sv, hmult_done[(s, ed, tb4 * 4 + 3)])
                    for dh in range(2):
                        yk = (s, ed, ttb - 1, dh) if ttb >= 1 else \
                             (s, ed - 1, TT - 1, dh)
                        if yk in yadd_done:
                            wait("tensor", sv, yadd_done[yk])
                        vtd = None
                        for ic2 in range(4):
                            sem = st_ if ic2 == 3 else None
                            vtd = run("tensor", lambda e, i=ic2, d=dh, t=ttb,
                                      w=ed % 2: e.matmul(
                                out=pw[:, d * 512:(d + 1) * 512],
                                lhsT=hm[w][:, i * SEG + t * 128:i * SEG + (t + 1) * 128],
                                rhs=wd_sb[w][:, i * 1024 + d * 512:i * 1024 + (d + 1) * 512],
                                start=(i == 0), stop=(i == 3)), sem) or vtd
                        down_done[(s, ed, ttb, dh)] = vtd
                if dc is not None:
                    n = k
                    col = dc * 16 + n
                    ki = scan_idx[0]
                    scan_idx[0] += 1
                    ok = (s, dc, n - 12) if n >= 12 else \
                         ((s, dc - 1, n + 4) if dc >= 1 else (s - 1, 7, n + 4))
                    if ok in outproj_done:
                        wait("vector", st_, outproj_done[ok])
                    if ki >= 12 and ki - 12 < len(copy_chain) and copy_chain[ki - 12] > 0:
                        wait("vector", ss, copy_chain[ki - 12])
                    xv = xn_fm[slot][:, dc * SEG:(dc + 1) * SEG].rearrange(
                        "p (m two) -> p m two", two=2)
                    run("vector", lambda e, c=col, xe=xv[:, :, 0:1],
                        xo=xv[:, :, 1:2], ub=u_buf[n % 2]: e.scalar_tensor_tensor(
                        out=ub[:].rearrange("p (m o) -> p m o", o=1),
                        in0=xe, scalar=q_all[:, c:c + 1], in1=xo,
                        op0=ALU.mult, op1=ALU.add))
                    init = 0.0 if s == 0 else states[:, col:col + 1]
                    sbap = s_all[:, (n % 12) * 512:(n % 12 + 1) * 512]
                    vsc = run("vector", lambda e, sb=sbap, c=col, init=init,
                              ub=u_buf[n % 2]: e.tensor_tensor_scan(
                        out=sb, data0=q2_all[:, c:c + 1].broadcast_to([128, 512]),
                        data1=ub[:], initial=init, op0=ALU.mult, op1=ALU.add), sv)
                    scan_done[(s, dc, n)] = vsc
                if dc is not None:
                    n = k
                    if s < NSEG - 1:
                        wait("scalar", sv, scan_done[(s, dc, n)])
                        sbap = s_all[:, (n % 12) * 512:(n % 12 + 1) * 512]
                        col = dc * 16 + n
                        cv = run("scalar", lambda e, sb=sbap, c=col: e.activation(
                            out=states[:, c:c + 1], in_=sb[:, 511:512],
                            func=AF.Copy), ss)
                        copy_chain.append(cv)
                    else:
                        copy_chain.append(0)

                if eg is not None and k % 2 == 1:
                    j = (k - 1) // 2
                    jb = j % 2
                    vtg, vtu = pair_done[(s, eg, j)]
                    wait("scalar", st_, vtg)
                    hk = (s, eg, j - 2) if j >= 2 else (s, eg - 1, 6 + j)
                    if hk in hmult_done:
                        wait("scalar", sv, hmult_done[hk])
                    vsl = run("scalar", lambda e, jb=jb: e.activation(
                        out=g_sb[jb][:], in_=pg[jb][:], func=AF.Silu), ss)
                    silu_done[(s, eg, j)] = vsl
                if dct is not None and k % 2 == 0 and ablate != "ssm":
                    jj = k // 2
                    if jj == 0:
                        wait("vector", sxt, sstm_done[(s, dct)])
                    sem = sv if jj == 7 else None
                    va = run("vector", lambda e, d=dct, t=jj: e.tensor_tensor(
                        out=y_acc[:, t * 1024 + d * 128:t * 1024 + (d + 1) * 128],
                        in0=ssm_tm[d % 2][:, t * 128:(t + 1) * 128],
                        in1=y_acc[:, t * 1024 + d * 128:t * 1024 + (d + 1) * 128],
                        op=ALU.add), sem)
                    if jj == 7:
                        sadd_done[(s, dct)] = va
                if ed is not None and k % 2 == 1:
                    ttb = (k - 1) // 2
                    for dh in range(2):
                        wait("vector", st_, down_done[(s, ed, ttb, dh)])
                        ysl = y_acc[:, ttb * 1024 + dh * 512:ttb * 1024 + (dh + 1) * 512]
                        if ablate == "ffn":
                            vya = run("vector", lambda e, ysl=ysl: e.tensor_scalar(
                                out=ysl, in0=ysl, scalar1=1.0, scalar2=None,
                                op0=ALU.mult), sv)
                            yadd_done[(s, ed, ttb, dh)] = vya
                            continue
                        if ed < 8:
                            vya = run("vector", lambda e, t=ttb, e_=ed, d=dh,
                                      ysl=ysl: e.scalar_tensor_tensor(
                                out=ysl, in0=pw[:, d * 512:(d + 1) * 512],
                                scalar=w_tm[:, t * 8 + e_:t * 8 + e_ + 1],
                                in1=ysl, op0=ALU.mult, op1=ALU.add), sv)
                        else:
                            vya = run("vector", lambda e, d=dh, ysl=ysl: e.tensor_tensor(
                                out=ysl, in0=pw[:, d * 512:(d + 1) * 512],
                                in1=ysl, op=ALU.add), sv)
                        yadd_done[(s, ed, ttb, dh)] = vya
                if eg is not None and k % 2 == 1:
                    j = (k - 1) // 2
                    jb = j % 2
                    vtg, vtu = pair_done[(s, eg, j)]
                    wait("vector", ss, silu_done[(s, eg, j)])
                    wait("vector", st_, vtu)
                    dk = (s, eg - 2, TT - 1, 1) if eg >= 2 else (s - 1, eg + 8, TT - 1, 1)
                    if j == 0 and dk in down_done:
                        wait("vector", st_, down_done[dk])
                    tb, ic = divmod(j, 4)
                    vh = run("vector", lambda e, jb=jb, t=tb, i=ic, w=eg % 2:
                             e.tensor_tensor(
                        out=hm[w][:, i * SEG + t * 512:i * SEG + (t + 1) * 512],
                        in0=pu[jb][:], in1=g_sb[jb][:], op=ALU.mult), sv)
                    hmult_done[(s, eg, j)] = vh
            if dc is not None:
                wait("tensor", sv, scan_done[(s, dc, 15)])
                wait("tensor", ss, diag_done[(s, dc)])
                for n in range(12, 16):
                    run("tensor", lambda e, n=n, ab=dc % 2: e.matmul(
                        out=pm[:],
                        lhsT=diag_all[ab][:, n * 128:(n + 1) * 128],
                        rhs=s_all[:, (n % 12) * 512:(n % 12) * 512 + 512],
                        start=False, stop=(n == 15)))
                    vt = run("tensor", lambda e, n=n, ab=dc % 2: e.matmul(
                        out=pop1[:, 1:512],
                        lhsT=diag_all[ab][:, (16 + n) * 128:(17 + n) * 128],
                        rhs=s_all[:, (n % 12) * 512:(n % 12) * 512 + 511],
                        start=False, stop=(n == 15)), st_)
                    outproj_done[(s, dc, n)] = vt
                wait("scalar", st_, vt)
                pk = (s, dc - 2) if dc >= 2 else (s - 1, dc + 6)
                if pk in sstm_done:
                    wait("scalar", sxt, sstm_done[pk])
                sv_o = sacc_bf[dc % 2][:].rearrange("p (m two) -> p m two", two=2)
                run("scalar", lambda e, o=sv_o[:, :, 1:2]: e.activation(
                    out=o, in_=pm[:].rearrange("p (m o) -> p m o", o=1),
                    func=AF.Copy))
                ev = run("scalar", lambda e, o=sv_o[:, :, 0:1]: e.activation(
                    out=o, in_=pop1[:].rearrange("p (m o) -> p m o", o=1),
                    func=AF.Copy), ss)
                evac_done[(s, dc)] = ev
                if s > 0:
                    wait("vector", ss, ev)
                    ev = run("vector", lambda e, ab=dc % 2, d=dc: e.tensor_tensor(
                        out=sacc_bf[ab][:, 0:1], in0=sacc_bf[ab][:, 0:1],
                        in1=corr8[:, d:d + 1], op=ALU.add), sv)
                if dc + 1 < 8:
                    run("scalar", lambda e, d=dc + 1: e.activation(
                        out=om_diag[d % 2][:], in_=id_f32[:], func=AF.Copy,
                        scale=om_t[:, d:d + 1]))
                    od = run("scalar", lambda e, d=dc + 1: e.activation(
                        out=ko_diag[d % 2][:], in_=id_f32[:], func=AF.Copy,
                        scale=kom[:, d:d + 1]), ss)
                    omdiag_done[(s, dc + 1)] = od
                    build_diags(dc + 1)
                if s > 0:
                    wait("sync", sv, ev)
                else:
                    wait("sync", ss, ev)
                pk2 = (s, dc - 2) if dc >= 2 else (s - 1, dc + 6)
                if pk2 in sadd_done:
                    wait("sync", sv, sadd_done[pk2])
                elif ablate == "ssm":
                    pass
                for tt2 in range(TT):
                    run("sync", lambda e, ab=dc % 2, t=tt2: e.dma_start_transpose(
                        out=ssm_tm[ab][:, t * 128:(t + 1) * 128],
                        in_=sacc_bf[ab][:, t * 128:(t + 1) * 128]), sxt, 16)
                sstm_done[(s, dc)] = sxt.v
            if eg is not None:
                tgt, te = next_exp(s, eg)
                if tgt is not None and tgt not in w_gu_ready:
                    wait("gpsimd", st_, st_.v)
                    w_gu_ready[tgt] = weight_batch_gu(te, eg % 2)
            if ed is not None:
                tgt, te = next_exp(s, ed)
                if tgt is not None and tgt not in w_d_ready:
                    wait("gpsimd", st_, st_.v)
                    w_d_ready[tgt] = weight_batch_d(te, ed % 2)
            if s + 1 < NSEG and 1 <= r <= 8:
                emit_p1_a(s + 1, r - 1)
            if s + 1 < NSEG and r == 9:
                emit_p2_logits(s + 1)
        seg_lastmm[s] = st_.v

    def emit_stores(s):
        for ttb in range(TT):
            wait("sync", sv, yadd_done[(s, NE - 1, ttb, 1)])
            rows = s * SEG + ttb * 128
            run("sync", lambda e, r=rows, t=ttb: e.dma_start(
                out=y_h[r:r + 128, :], in_=y_acc[:, t * 1024:(t + 1) * 1024]),
                sst, 16)
        store_done[s + 1] = sst.v


    for tt in range(4):
        emit_p1_load(0, tt)
    for tt in range(TT):
        emit_p1_tile(0, tt)
    emit_reloads(0)
    emit_p2_logits(0)
    emit_p2_rest(0)
    for s in range(NSEG):
        emit_rounds(s)
        emit_stores(s)
        if s + 1 < NSEG:
            emit_reloads(s + 1)
            emit_p2_rest(s + 1)

    if debug:
        wait("sync", sv, sv.v)
        wait("sync", ss, ss.v)
        wait("sync", st_, st_.v)
        dkom_h = nc.dram_tensor("dkom", [128, 8], F32, kind="ExternalOutput").ap()
        dcq_h = nc.dram_tensor("dcq", [128, 128], F32, kind="ExternalOutput").ap()
        dcorr_h = nc.dram_tensor("dcorr", [128, 8], F32, kind="ExternalOutput").ap()
        for dst_ap, src_t in [(dq_h, q_all), (dc_h, c_all), (dst_h, states),
                              (dkom_h, kom), (dcq_h, cq_all), (dcorr_h, corr8),
                              (dwtm_h, w_tm), (dsfc_h, sfc_tm),
                              (dxn_h, xn_fm[1]), (dssm_h, sacc_bf[1]),
                              (dxs_h, x_stage[1]), (dxnst_h, xn_stage[1]),
                              (dbn_h, bn_s)]:
            run("sync", lambda e, d=dst_ap, s=src_t: e.dma_start(out=d, in_=s[:]),
                sst, 16)
    wait("sync", sst, sst.v)

    with nc.Block() as block:
        @block.sync
        def _(e):
            for f in steps["sync"]:
                f(e)

        @block.gpsimd
        def _(e):
            for f in steps["gpsimd"]:
                f(e)

        @block.vector
        def _(e):
            for f in steps["vector"]:
                f(e)

        @block.scalar
        def _(e):
            for f in steps["scalar"]:
                f(e)

        @block.tensor
        def _(e):
            for f in steps["tensor"]:
                f(e)

    ctx.close()
    return nc


def kernel(**inputs):
    def np32(a):
        return np.ascontiguousarray(np.asarray(a, dtype=np.float32))

    x = np32(inputs["L0_attn_x"])
    delta = np32(inputs["L0_attn_delta"])[:, :, 0]
    alpha = np32(inputs["L0_attn_alpha"])[:, :, 0]
    beta = np32(inputs["L0_attn_beta"])[:, :, 0]
    gamma = np32(inputs["L0_attn_gamma"])
    omega = np32(inputs["L0_attn_omega"])
    gate_w = np32(inputs["L0_ffn_gate_w"])
    gate_bias = np32(inputs["L0_ffn_gate_bias"])
    w_gate = np32(inputs["L0_ffn_w_gate"])
    w_up = np32(inputs["L0_ffn_w_up"])
    w_down = np32(inputs["L0_ffn_w_down"])
    sw_gate = np32(inputs["L0_ffn_sw_gate"])
    sw_up = np32(inputs["L0_ffn_sw_up"])
    sw_down = np32(inputs["L0_ffn_sw_down"])
    _dbg = inputs.get("_debug", False)

    def chunk128(a2d, inner):
        return np.ascontiguousarray(
            a2d.reshape(8, 128, inner).transpose(1, 0, 2).reshape(128, 8 * inner))

    shared = {
        "delta_p": chunk128(delta, 16), "alpha_p": chunk128(alpha, 16),
        "beta_p": chunk128(beta, 16), "gamma_p": chunk128(gamma, 16),
        "omega_p": np.ascontiguousarray(omega.reshape(8, 128).T),
        "gw_p": chunk128(gate_w, 8),
        "gb_p": np.ascontiguousarray(gate_bias.reshape(8, 1)),
        "w_gate": w_gate, "w_up": w_up, "w_down": w_down,
        "sw_gate": sw_gate, "sw_up": sw_up, "sw_down": sw_down,
    }
    nc = build_nc(debug=_dbg, ablate=inputs.get('_ablate'))
    in_maps = [dict(shared, x=np.ascontiguousarray(x[b])) for b in range(B)]
    res = run_bass_kernel_spmd(nc, in_maps, core_ids=list(range(B)))
    if _dbg:
        return res
    out = np.stack([res.results[b]["y"] for b in range(B)], axis=0)
    return out.astype(np.float32)


if __name__ == "__main__":
    import reference as ref
    inp = ref.setup_inputs(0)
    want = np.asarray(ref.reference(**inp))
    got = kernel(**{k: np.asarray(v) for k, v in inp.items()})
    err = float(np.mean((got - want) ** 2) / np.mean(want ** 2))
    print("residual variance ratio:", err)


# version 25
# speedup vs baseline: 1.0034x; 1.0034x over previous
import contextlib
import numpy as np
import concourse.bass as bass
import concourse.mybir as mybir
from concourse.bass_utils import run_bass_kernel_spmd

F32 = mybir.dt.float32
BF16 = mybir.dt.bfloat16
AF = mybir.ActivationFunctionType
ALU = mybir.AluOpType

B, L, D, NM = 8, 4096, 1024, 16
SEG = 1024
NSEG = L // SEG
TT = SEG // 128
DC = D // 128
NE = 10
NTB = SEG // 512
ROUTED_SCALE = 2.5


class Sem:
    def __init__(self, h):
        self.h = h
        self.v = 0

    def inc(self, n=1):
        self.v += n
        return self.v


def build_nc(debug=False, ablate=None):
    nc = bass.Bass("TRN2", target_bir_lowering=False)

    x_h = nc.dram_tensor("x", [L, D], F32, kind="ExternalInput").ap()
    y_h = nc.dram_tensor("y", [L, D], F32, kind="ExternalOutput").ap()
    if debug:
        dq_h = nc.dram_tensor("dq", [128, 128], F32, kind="ExternalOutput").ap()
        dc_h = nc.dram_tensor("dc", [128, 128], F32, kind="ExternalOutput").ap()
        dst_h = nc.dram_tensor("dst", [128, 128], F32, kind="ExternalOutput").ap()
        dwtm_h = nc.dram_tensor("dwtm", [128, 64], F32, kind="ExternalOutput").ap()
        dsfc_h = nc.dram_tensor("dsfc", [128, 64], F32, kind="ExternalOutput").ap()
        dxn_h = nc.dram_tensor("dxn", [128, 8192], mybir.dt.bfloat16, kind="ExternalOutput").ap()
        dssm_h = nc.dram_tensor("dssm", [128, 1024], mybir.dt.bfloat16, kind="ExternalOutput").ap()
        dxs_h = nc.dram_tensor("dxs", [128, 1024], F32, kind="ExternalOutput").ap()
        dxnst_h = nc.dram_tensor("dxnst", [128, 1024], mybir.dt.bfloat16, kind="ExternalOutput").ap()
        dbn_h = nc.dram_tensor("dbn", [128, 24], F32, kind="ExternalOutput").ap()
    dl_h = nc.dram_tensor("delta_p", [128, 128], F32, kind="ExternalInput").ap()
    al_h = nc.dram_tensor("alpha_p", [128, 128], F32, kind="ExternalInput").ap()
    bt_h = nc.dram_tensor("beta_p", [128, 128], F32, kind="ExternalInput").ap()
    gm_h = nc.dram_tensor("gamma_p", [128, 128], F32, kind="ExternalInput").ap()
    om_h = nc.dram_tensor("omega_p", [128, 8], F32, kind="ExternalInput").ap()
    gw_h = nc.dram_tensor("gw_p", [128, 64], F32, kind="ExternalInput").ap()
    gb_h = nc.dram_tensor("gb_p", [8, 1], F32, kind="ExternalInput").ap()
    wg_h = nc.dram_tensor("w_gate", [8, D, 512], F32, kind="ExternalInput").ap()
    wu_h = nc.dram_tensor("w_up", [8, D, 512], F32, kind="ExternalInput").ap()
    wd_h = nc.dram_tensor("w_down", [8, 512, D], F32, kind="ExternalInput").ap()
    swg_h = nc.dram_tensor("sw_gate", [D, D], F32, kind="ExternalInput").ap()
    swu_h = nc.dram_tensor("sw_up", [D, D], F32, kind="ExternalInput").ap()
    swd_h = nc.dram_tensor("sw_down", [D, D], F32, kind="ExternalInput").ap()

    ctx = contextlib.ExitStack()
    E = ctx.enter_context

    y_acc = E(nc.sbuf_tensor("y_acc", [128, TT * 1024], F32))
    xn_fm = [E(nc.sbuf_tensor(f"xn_fm{i}", [128, DC * SEG], BF16)) for i in range(2)]
    wg_sb = [E(nc.sbuf_tensor(f"wg{i}", [128, 8 * 512], BF16)) for i in range(2)]
    wu_sb = [E(nc.sbuf_tensor(f"wu{i}", [128, 8 * 512], BF16)) for i in range(2)]
    wd_sb = [E(nc.sbuf_tensor(f"wd{i}", [128, 4 * 1024], BF16)) for i in range(2)]
    s_all = E(nc.sbuf_tensor("s_all", [128, 12 * 512], BF16))
    sacc_bf = [E(nc.sbuf_tensor(f"saccb{i}", [128, SEG], BF16)) for i in range(2)]
    ssm_tm = [E(nc.sbuf_tensor(f"ssmtm{i}", [128, SEG], BF16)) for i in range(2)]
    diag_all = [E(nc.sbuf_tensor(f"diag_all{i}", [128, 32 * 128], BF16)) for i in range(2)]
    ko_diag = [E(nc.sbuf_tensor(f"kodg{i}", [128, 128], BF16)) for i in range(2)]
    u_buf = [E(nc.sbuf_tensor(f"ubuf{i}", [128, 512], F32)) for i in range(2)]
    q2_all = E(nc.sbuf_tensor("q2_all", [128, 128], F32))
    cq_all = E(nc.sbuf_tensor("cq_all", [128, 128], F32))
    kom = E(nc.sbuf_tensor("kom", [128, 8], F32))
    corr8 = E(nc.sbuf_tensor("corr8", [128, 8], F32))
    cscr = E(nc.sbuf_tensor("cscr", [128, 128], F32))
    om_diag = [E(nc.sbuf_tensor(f"omdg{i}", [128, 128], BF16)) for i in range(2)]
    hm = [E(nc.sbuf_tensor(f"hm{i}", [128, 4 * SEG], BF16)) for i in range(2)]
    g_sb = [E(nc.sbuf_tensor(f"gsb{i}", [128, 512], BF16)) for i in range(2)]
    x_stage = [E(nc.sbuf_tensor(f"xstg{i}", [128, 1024], F32)) for i in range(4)]
    xn_stage = [E(nc.sbuf_tensor(f"xnstg{i}", [128, 1024], BF16)) for i in range(4)]
    bn_s = E(nc.sbuf_tensor("bn_s", [128, 24], F32))
    p_t = E(nc.sbuf_tensor("p_t", [128, 128], F32))
    a_t = E(nc.sbuf_tensor("a_t", [128, 128], F32))
    q_all = E(nc.sbuf_tensor("q_all", [128, 128], F32))
    c_all = E(nc.sbuf_tensor("c_all", [128, 128], F32))
    bt_t = E(nc.sbuf_tensor("bt_t", [128, 128], F32))
    gm_t = E(nc.sbuf_tensor("gm_t", [128, 128], F32))
    om_t = E(nc.sbuf_tensor("om_t", [128, 8], F32))
    gwt = E(nc.sbuf_tensor("gwt", [128, 64], F32))
    gwt_bf = E(nc.sbuf_tensor("gwt_bf", [128, 64], BF16))
    gbt = E(nc.sbuf_tensor("gbt", [8, 1], F32))
    states = E(nc.sbuf_tensor("states", [128, 128], F32))
    id_f32 = E(nc.sbuf_tensor("id_f32", [128, 128], F32))
    id_int = E(nc.sbuf_tensor("id_int", [128, 128], mybir.dt.int32))
    sfc_fm = E(nc.sbuf_tensor("sfc_fm", [8, SEG], F32))
    sfc_tm = E(nc.sbuf_tensor("sfc_tm", [128, TT * 8], F32))
    gs_tm = E(nc.sbuf_tensor("gs_tm", [128, TT * 4], F32))
    rnk = E(nc.sbuf_tensor("rnk", [128, TT * 8], F32))
    selg = E(nc.sbuf_tensor("selg", [128, TT * 4], F32))
    tmp_e = E(nc.sbuf_tensor("tmp_e", [128, TT * 8], F32))
    w_tm = E(nc.sbuf_tensor("w_tm", [128, TT * 8], F32))
    scr = E(nc.sbuf_tensor("scr", [128, TT * 8], F32))
    gscr = E(nc.sbuf_tensor("gscr", [128, TT * 4], F32))
    den = E(nc.sbuf_tensor("den", [128, TT], F32))

    pg = [E(nc.psum_tensor(f"pg{i}", [128, 512], F32)) for i in range(2)]
    pu = [E(nc.psum_tensor(f"pu{i}", [128, 512], F32)) for i in range(2)]
    pw = E(nc.psum_tensor("pw", [128, 1024], F32))
    pm = E(nc.psum_tensor("pm", [128, 512], F32))
    pop1 = E(nc.psum_tensor("pop1", [128, 512], F32))

    sx = Sem(E(nc.semaphore("sx")))
    sxt = Sem(E(nc.semaphore("sxt")))
    sw = Sem(E(nc.semaphore("sw")))
    sst = Sem(E(nc.semaphore("sst")))
    sv = Sem(E(nc.semaphore("sv")))
    ss = Sem(E(nc.semaphore("ss")))
    st_ = Sem(E(nc.semaphore("st")))
    sg = Sem(E(nc.semaphore("sg")))

    steps = {"sync": [], "gpsimd": [], "vector": [], "scalar": [], "tensor": []}

    def emit(eng, fn):
        steps[eng].append(fn)

    def wait(eng, sem, val):
        if val > 0:
            emit(eng, lambda e, s=sem, v=val: e.wait_ge(s.h, v))

    def run(eng, fn, sem=None, n=1):
        if sem is None:
            emit(eng, lambda e, fn=fn: fn(e))
            return None
        v = sem.inc(n)
        emit(eng, lambda e, fn=fn, s=sem, n=n: fn(e).then_inc(s.h, n))
        return v

    run("gpsimd", lambda e: e.iota(out=id_int[:], pattern=[[1, 128]], base=0,
                                   channel_multiplier=-1))
    run("gpsimd", lambda e: e.tensor_scalar(out=id_int[:], in0=id_int[:],
                                            scalar1=0, scalar2=None,
                                            op0=ALU.is_equal))
    v_id = run("gpsimd", lambda e: e.tensor_copy(out=id_f32[:], in_=id_int[:]), sg)

    for dst, src in [(p_t, dl_h), (a_t, al_h), (bt_t, bt_h), (gm_t, gm_h),
                     (om_t, om_h), (gwt, gw_h), (gbt, gb_h)]:
        run("sync", lambda e, d=dst, s=src: e.dma_start(out=d[:], in_=s), sx, 16)
    v_params = sx.v

    def weight_batch_gu(eidx, slot):
        for dc in range(8):
            if eidx < 8:
                src_g = wg_h[eidx, dc * 128:(dc + 1) * 128, :]
                src_u = wu_h[eidx, dc * 128:(dc + 1) * 128, :]
            else:
                c0 = (eidx - 8) * 512
                src_g = swg_h[dc * 128:(dc + 1) * 128, c0:c0 + 512]
                src_u = swu_h[dc * 128:(dc + 1) * 128, c0:c0 + 512]
            run("gpsimd", lambda e, d=wg_sb[slot][:, dc * 512:(dc + 1) * 512],
                s=src_g: e.dma_start(out=d, in_=s), sw, 16)
            run("gpsimd", lambda e, d=wu_sb[slot][:, dc * 512:(dc + 1) * 512],
                s=src_u: e.dma_start(out=d, in_=s), sw, 16)
        return sw.v

    def weight_batch_d(eidx, slot):
        for ic in range(4):
            if eidx < 8:
                src_d = wd_h[eidx, ic * 128:(ic + 1) * 128, :]
            else:
                r0 = (eidx - 8) * 512 + ic * 128
                src_d = swd_h[r0:r0 + 128, :]
            run("gpsimd", lambda e, d=wd_sb[slot][:, ic * 1024:(ic + 1) * 1024],
                s=src_d: e.dma_start(out=d, in_=s), sw, 16)
        return sw.v

    def next_exp(s_, e_):
        if e_ + 2 < NE:
            return (s_, e_ + 2), e_ + 2
        if s_ + 1 < NSEG:
            return (s_ + 1, e_ + 2 - NE), e_ + 2 - NE
        return None, None

    w_gu_ready = {}
    w_d_ready = {}
    w_gu_ready[(0, 0)] = weight_batch_gu(0, 0)
    w_d_ready[(0, 0)] = weight_batch_d(0, 0)
    w_gu_ready[(0, 1)] = weight_batch_gu(1, 1)
    w_d_ready[(0, 1)] = weight_batch_d(1, 1)

    wait("scalar", sx, v_params)
    run("scalar", lambda e: e.activation(out=p_t[:], in_=p_t[:], func=AF.Sigmoid))
    v_sig = run("scalar", lambda e: e.activation(out=a_t[:], in_=a_t[:],
                                                 func=AF.Sigmoid), ss)
    wait("vector", ss, v_sig)
    run("vector", lambda e: e.tensor_tensor(out=q_all[:], in0=p_t[:], in1=a_t[:],
                                            op=ALU.mult))
    run("vector", lambda e: e.tensor_tensor(out=c_all[:], in0=p_t[:], in1=bt_t[:],
                                            op=ALU.mult))
    run("vector", lambda e: e.tensor_scalar(out=q_all[:], in0=q_all[:],
                                            scalar1=-1.0, scalar2=1.0,
                                            op0=ALU.mult, op1=ALU.add))
    run("vector", lambda e: e.tensor_tensor(out=c_all[:], in0=c_all[:],
                                            in1=gm_t[:], op=ALU.mult))
    run("vector", lambda e: e.tensor_scalar(
        out=c_all[:], in0=c_all[:], scalar1=0.25, scalar2=None, op0=ALU.mult))
    run("vector", lambda e: e.tensor_tensor(out=q2_all[:], in0=q_all[:],
                                            in1=q_all[:], op=ALU.mult))
    for d2 in range(8):
        run("vector", lambda e, d=d2: e.reduce_sum(
            out=kom[:, d:d + 1], in_=c_all[:, d * 16:(d + 1) * 16],
            axis=mybir.AxisListType.X))
    run("vector", lambda e: e.tensor_tensor(out=cq_all[:], in0=c_all[:],
                                            in1=q_all[:], op=ALU.mult))
    run("vector", lambda e: e.tensor_copy(out=gwt_bf[:], in_=gwt[:]))
    v_ssmparams = run("vector", lambda e: e.tensor_tensor(
        out=kom[:], in0=kom[:], in1=om_t[:], op=ALU.add), sv)

    ln_done = {}
    xt_done = {}
    store_done = {0: 0}
    tr_fill_done = {}
    tr_add_done = {}
    hmult_done = {}
    yadd_done = {}
    pair_done = {}
    silu_done = {}
    down_done = {}
    outproj_done = {}
    evac_done = {}
    omdiag_done = {}
    sstm_done = {}
    sadd_done = {}
    diag_done = {}
    scan_done = {}
    seg_lastmm = {}
    gate_done = {}
    gate_evac = {}
    copy_chain = []
    scan_idx = [0]

    seg_xt_map = {}
    xreload_map = {}

    x_load_done = {}

    def emit_p1_load(s, tt):
        rows = s * SEG + tt * 128
        b = tt % 4
        if tt >= 4:
            wait("sync", sv, ln_done[(s, tt - 4)])
        x_load_done[(s, tt)] = run("sync", lambda e, r=rows, b=b: e.dma_start(
            out=x_stage[b][:], in_=x_h[r:r + 128, :]), sx, 16)

    p1_vs = {}

    def emit_p1_a(s, tt):
        slot = s % 2
        b = tt % 4
        if (s, tt) not in x_load_done:
            emit_p1_load(s, tt)
        wait("vector", sx, x_load_done[(s, tt)])
        if tt >= 4:
            wait("vector", sxt, xt_done[(s, tt - 4)])
        run("vector", lambda e, b=b: e.bn_stats(out=bn_s[:, 0:6],
                                                in_=x_stage[b][:, 0:512]))
        run("vector", lambda e, b=b: e.bn_stats(out=bn_s[:, 6:12],
                                                in_=x_stage[b][:, 512:1024]))
        def vtt(o, i0_, i1_, op):
            run("vector", lambda e, o=o, a=i0_, b2=i1_, op=op: e.tensor_tensor(
                out=bn_s[:, o:o + 1], in0=bn_s[:, a:a + 1],
                in1=bn_s[:, b2:b2 + 1], op=op))
        vtt(12, 1, 4, ALU.add)
        vtt(13, 7, 10, ALU.add)
        vtt(14, 2, 5, ALU.add)
        vtt(15, 8, 11, ALU.add)
        vtt(16, 1, 1, ALU.mult)
        vtt(17, 4, 4, ALU.mult)
        vtt(18, 7, 7, ALU.mult)
        vtt(19, 10, 10, ALU.mult)
        vtt(12, 12, 13, ALU.add)
        vtt(14, 14, 15, ALU.add)
        vtt(16, 16, 17, ALU.add)
        vtt(18, 18, 19, ALU.add)
        vtt(11, 11, 11, ALU.max)
        run("vector", lambda e: e.tensor_scalar(out=bn_s[:, 12:13],
            in0=bn_s[:, 12:13], scalar1=0.25, scalar2=None, op0=ALU.mult))
        vtt(16, 16, 18, ALU.add)
        vtt(11, 11, 11, ALU.max)
        vtt(20, 12, 12, ALU.mult)
        run("vector", lambda e: e.scalar_tensor_tensor(out=bn_s[:, 14:15],
            in0=bn_s[:, 16:17], scalar=256.0, in1=bn_s[:, 14:15],
            op0=ALU.mult, op1=ALU.add))
        vtt(13, 13, 13, ALU.add)
        vtt(11, 11, 11, ALU.max)
        run("vector", lambda e: e.scalar_tensor_tensor(out=bn_s[:, 21:22],
            in0=bn_s[:, 14:15], scalar=1.0 / 1024.0, in1=bn_s[:, 20:21],
            op0=ALU.mult, op1=ALU.subtract))
        vtt(15, 15, 15, ALU.add)
        vtt(11, 11, 11, ALU.max)
        vv = run("vector", lambda e: e.tensor_scalar(
            out=bn_s[:, 21:22], in0=bn_s[:, 21:22], scalar1=1e-5,
            scalar2=None, op0=ALU.add), sv)
        wait("scalar", sv, vv)
        vs = run("scalar", lambda e: e.activation(
            out=bn_s[:, 22:23], in_=bn_s[:, 21:22], func=AF.Sqrt), ss)
        p1_vs[(s, tt)] = vs

    def emit_p1_b(s, tt):
        slot = s % 2
        b = tt % 4
        vs = p1_vs[(s, tt)]
        wait("vector", ss, vs)
        vv = run("vector", lambda e: e.reciprocal(out=bn_s[:, 23:24],
                                                  in_=bn_s[:, 22:23]), sv)
        wait("vector", sv, vv)
        run("vector", lambda e: e.tensor_tensor(
            out=bn_s[:, 11:12], in0=bn_s[:, 11:12], in1=bn_s[:, 11:12],
            op=ALU.max))
        vln = run("vector", lambda e, b=b: e.tensor_scalar(
            out=xn_stage[b][:], in0=x_stage[b][:],
            scalar1=bn_s[:, 12:13], scalar2=bn_s[:, 23:24],
            op0=ALU.subtract, op1=ALU.mult), sv)
        ln_done[(s, tt)] = vln
        wait("sync", sv, vln)
        if s >= 2 and tt == 0:
            wait("sync", st_, seg_lastmm[s - 2])
        for dc in range(DC):
            run("sync", lambda e, d=dc, t=tt, b=b, sl=slot: e.dma_start_transpose(
                out=xn_fm[sl][:, d * SEG + t * 128:d * SEG + (t + 1) * 128],
                in_=xn_stage[b][:, d * 128:(d + 1) * 128]), sxt, 16)
        xt_done[(s, tt)] = sxt.v
        if tt == TT - 1:
            seg_xt_map[s] = sxt.v

    def emit_reloads(s):
        for tt in range(TT):
            rows = s * SEG + tt * 128
            wait("sync", sst, store_done[s])
            run("sync", lambda e, r=rows, t=tt: e.dma_start(
                out=y_acc[:, t * 1024:(t + 1) * 1024], in_=x_h[r:r + 128, :]),
                sx, 16)
        xreload_map[s] = sx.v

    p2_sig = {}


    def emit_p1_tile(s, tt):
        emit_p1_a(s, tt)
        emit_p1_b(s, tt)

    def emit_p2_logits(s):
        slot = s % 2
        wait("tensor", sx, v_params)
        if s == 0:
            wait("tensor", sg, v_id)
            wait("tensor", sv, v_ssmparams)
        else:
            wait("tensor", ss, evac_done[(s - 1, 7)])
        for tb in range(NTB):
            wait("tensor", sxt, xt_done[(s, tb * 4 + 3)])
            vt = None
            for dc in range(DC):
                sem = st_ if dc == 7 else None
                vt = run("tensor", lambda e, d=dc, t=tb, sl=slot: e.matmul(
                    out=pm[0:8, 0:512],
                    lhsT=gwt_bf[:, d * 8:(d + 1) * 8],
                    rhs=xn_fm[sl][:, d * SEG + t * 512:d * SEG + (t + 1) * 512],
                    start=(d == 0), stop=(d == 7)), sem) or vt
            wait("scalar", st_, vt)
            vs = run("scalar", lambda e, t=tb: e.activation(
                out=sfc_fm[0:8, t * 512:(t + 1) * 512], in_=pm[0:8, 0:512],
                func=AF.Sigmoid), ss)
            wait("tensor", ss, vs)
        p2_sig[s] = ss.v

    def emit_p2_rest(s):
        slot = s % 2
        wait("vector", ss, p2_sig[s])
        vv = run("vector", lambda e: e.tensor_scalar(
            out=sfc_fm[0:8, :], in0=sfc_fm[0:8, :], scalar1=gbt[0:8, 0:1],
            scalar2=None, op0=ALU.add), sv)
        wait("tensor", sv, vv)
        vt = None
        for tt in range(TT):
            sem = st_ if tt == TT - 1 else None
            vt = run("tensor", lambda e, t=tt: e.transpose(
                out=pm[:, 64 + t * 8:64 + (t + 1) * 8],
                in_=sfc_fm[0:8, t * 128:(t + 1) * 128],
                identity=id_f32[0:8, 0:8]), sem) or vt
        wait("scalar", st_, vt)
        ec1 = run("scalar", lambda e: e.copy(out=gbt[:], in_=gbt[:]), ss)
        ec2 = run("scalar", lambda e: e.copy(out=gbt[:], in_=gbt[:]), ss)
        wait("vector", st_, vt)
        wait("vector", ss, ec2)
        for _ in range(4):
            run("vector", lambda e: e.tensor_tensor(
                out=den[:], in0=den[:], in1=den[:], op=ALU.max))
        ge = run("vector", lambda e: e.tensor_copy(
            out=sfc_tm[:], in_=pm[:, 64:64 + TT * 8]), sv)
        gate_evac[s] = ge

        def ecol(tile, e_):
            return tile[:].rearrange("p (t e) -> p t e", e=8)[:, :, e_:e_ + 1]

        def gcol(tile, g_):
            return tile[:].rearrange("p (t g) -> p t g", g=4)[:, :, g_:g_ + 1]

        den3 = den[:].rearrange("p (t o) -> p t o", o=1)

        def filler(k=1):
            for _ in range(k):
                run("vector", lambda e: e.tensor_tensor(
                    out=ecol(scr, 0), in0=ecol(scr, 0), in1=ecol(scr, 0),
                    op=ALU.max))

        filler(2)
        for g in range(4):
            run("vector", lambda e, g=g: e.tensor_tensor(
                out=gcol(gs_tm, g), in0=ecol(sfc_tm, 2 * g),
                in1=ecol(sfc_tm, 2 * g + 1), op=ALU.add))
        gothers = {g: [gp for gp in range(4) if gp != g] for g in range(4)}

        def gcmp(dst, g, gp):
            op = ALU.is_ge if gp < g else ALU.is_gt
            run("vector", lambda e, d=dst, g=g, gp=gp, op=op: e.tensor_tensor(
                out=gcol(d, g), in0=gcol(gs_tm, gp), in1=gcol(gs_tm, g), op=op))

        for g in range(4):
            gcmp(selg, g, gothers[g][0])
        for g in range(4):
            gcmp(gscr, g, gothers[g][1])
        for g in range(4):
            run("vector", lambda e, g=g: e.tensor_tensor(
                out=gcol(selg, g), in0=gcol(selg, g), in1=gcol(gscr, g),
                op=ALU.add))
        for g in range(4):
            gcmp(gscr, g, gothers[g][2])
        for g in range(4):
            run("vector", lambda e, g=g: e.tensor_tensor(
                out=gcol(selg, g), in0=gcol(selg, g), in1=gcol(gscr, g),
                op=ALU.add))
        for g in range(4):
            run("vector", lambda e, g=g: e.tensor_scalar(
                out=gcol(selg, g), in0=gcol(selg, g), scalar1=2.0,
                scalar2=None, op0=ALU.is_lt))
        for e_ in range(8):
            run("vector", lambda e, e_=e_: e.tensor_tensor(
                out=ecol(tmp_e, e_), in0=ecol(sfc_tm, e_),
                in1=gcol(selg, e_ // 2), op=ALU.mult))
        eothers = {e_: [ep for ep in range(8) if ep != e_] for e_ in range(8)}

        def ecmp(dst, e_, ep):
            op = ALU.is_ge if ep < e_ else ALU.is_gt
            run("vector", lambda e, d=dst, e_=e_, ep=ep, op=op: e.tensor_tensor(
                out=ecol(d, e_), in0=ecol(tmp_e, ep), in1=ecol(tmp_e, e_), op=op))

        for e_ in range(8):
            ecmp(rnk, e_, eothers[e_][0])
        for r in range(1, 7):
            for e_ in range(8):
                ecmp(scr, e_, eothers[e_][r])
            for e_ in range(8):
                run("vector", lambda e, e_=e_: e.tensor_tensor(
                    out=ecol(rnk, e_), in0=ecol(rnk, e_), in1=ecol(scr, e_),
                    op=ALU.add))
        for e_ in range(8):
            run("vector", lambda e, e_=e_: e.tensor_scalar(
                out=ecol(rnk, e_), in0=ecol(rnk, e_), scalar1=2.0,
                scalar2=None, op0=ALU.is_lt))
        for e_ in range(8):
            run("vector", lambda e, e_=e_: e.tensor_tensor(
                out=ecol(w_tm, e_), in0=ecol(sfc_tm, e_),
                in1=ecol(rnk, e_), op=ALU.mult))
        for k in range(4):
            run("vector", lambda e, k=k: e.tensor_tensor(
                out=gcol(gscr, k), in0=ecol(w_tm, 2 * k),
                in1=ecol(w_tm, 2 * k + 1), op=ALU.add))
        run("vector", lambda e: e.tensor_tensor(
            out=gcol(selg, 0), in0=gcol(gscr, 0), in1=gcol(gscr, 1), op=ALU.add))
        filler(1)
        run("vector", lambda e: e.tensor_tensor(
            out=gcol(selg, 1), in0=gcol(gscr, 2), in1=gcol(gscr, 3), op=ALU.add))
        filler(2)
        run("vector", lambda e: e.tensor_tensor(
            out=den3, in0=gcol(selg, 0), in1=gcol(selg, 1), op=ALU.add))
        filler(2)
        run("vector", lambda e: e.tensor_scalar(
            out=den[:], in0=den[:], scalar1=1.0 / ROUTED_SCALE, scalar2=1e-20,
            op0=ALU.mult, op1=ALU.add))
        filler(2)
        run("vector", lambda e: e.reciprocal(out=den[:], in_=den[:]))
        filler(3)
        gd = None
        for e_ in range(8):
            sem = sv if e_ == 7 else None
            gd = run("vector", lambda e, e_=e_: e.tensor_tensor(
                out=ecol(w_tm, e_), in0=ecol(w_tm, e_),
                in1=den3, op=ALU.mult), sem) or gd
        gate_done[s] = gd

    def emit_rounds(s):
        slot = s % 2
        seg_xt = seg_xt_map[s]
        xreload = xreload_map[s]
        wait("vector", sxt, seg_xt)
        wait("vector", sx, xreload)
        wait("scalar", sx, v_params)
        if s == 0:
            wait("scalar", sg, v_id)
        run("scalar", lambda e: e.activation(
            out=om_diag[0][:], in_=id_f32[:], func=AF.Copy,
            scale=om_t[:, 0:1]), ss)
        od = run("scalar", lambda e: e.activation(
            out=ko_diag[0][:], in_=id_f32[:], func=AF.Copy,
            scale=kom[:, 0:1]), ss)
        omdiag_done[(s, 0)] = od

        def build_diags(dc2):
            dk2 = (s, dc2 - 2, 15) if dc2 >= 2 else (s - 1, dc2 + 6, 15)
            if dk2 in outproj_done:
                wait("scalar", st_, outproj_done[dk2])
            dv = None
            for n2 in range(NM):
                run("scalar", lambda e, n=n2, c=dc2 * 16 + n2, ab=dc2 % 2:
                    e.activation(
                    out=diag_all[ab][:, n * 128:(n + 1) * 128],
                    in_=id_f32[:], func=AF.Copy, scale=c_all[:, c:c + 1]))
                sem = ss if n2 == NM - 1 else None
                dv = run("scalar", lambda e, n=n2, c=dc2 * 16 + n2, ab=dc2 % 2:
                         e.activation(
                    out=diag_all[ab][:, (16 + n) * 128:(17 + n) * 128],
                    in_=id_f32[:], func=AF.Copy, scale=cq_all[:, c:c + 1]),
                    sem) or dv
            diag_done[(s, dc2)] = dv

        build_diags(0)
        if s > 0:
            wait("vector", ss, ss.v)
            run("vector", lambda e: e.tensor_tensor(
                out=cscr[:], in0=cq_all[:], in1=states[:], op=ALU.mult))
            run("vector", lambda e: e.tensor_tensor(
                out=bn_s[:, 11:12], in0=bn_s[:, 11:12], in1=bn_s[:, 11:12],
                op=ALU.max))
            run("vector", lambda e: e.tensor_tensor(
                out=bn_s[:, 11:12], in0=bn_s[:, 11:12], in1=bn_s[:, 11:12],
                op=ALU.max))
            for d2 in range(8):
                run("vector", lambda e, d=d2: e.reduce_sum(
                    out=corr8[:, d:d + 1], in_=cscr[:, d * 16:(d + 1) * 16],
                    axis=mybir.AxisListType.X))
        for r in range(11):
            dc = r if r < 8 else None
            if s + 1 < NSEG and r < 8 and (s + 1, r) not in x_load_done:
                emit_p1_load(s + 1, r)
            eg = r if r < 10 else None
            ed = r - 1 if r >= 1 else None
            dct = r - 2 if 2 <= r <= 9 else None
            if eg is not None:
                wait("tensor", sw, w_gu_ready[(s, eg)])
            if ed is not None:
                wait("tensor", sw, w_d_ready[(s, ed)])
            for k in range(16):
                if k == 5 and s + 1 < NSEG and 2 <= r <= 9:
                    emit_p1_b(s + 1, r - 2)
                if dc is not None and k == 0:
                    wait("tensor", ss, omdiag_done[(s, dc)])
                    pk = (s, dc - 1) if dc >= 1 else (s - 1, 7)
                    if pk in evac_done:
                        wait("tensor", ss, evac_done[pk])
                    if dc == 0:
                        wait("tensor", sv, gate_evac[s])
                    xodd = xn_fm[slot][:, dc * SEG:(dc + 1) * SEG].rearrange(
                        "p (m two) -> p m two", two=2)[:, :, 1:2]
                    xeven = xn_fm[slot][:, dc * SEG:(dc + 1) * SEG].rearrange(
                        "p (m two) -> p m two", two=2)[:, :, 0:1]
                    run("tensor", lambda e, d=dc, x=xodd: e.matmul(
                        out=pm[:], lhsT=om_diag[d % 2][:], rhs=x,
                        start=True, stop=False))
                    run("tensor", lambda e, d=dc, x=xeven: e.matmul(
                        out=pop1[:], lhsT=ko_diag[d % 2][:], rhs=x,
                        start=True, stop=False))
                bslots = (11, 13, 15) if r == 0 else (5, 9, 13)
                if dc is not None and k in bslots:
                    g = bslots.index(k)
                    n_hi = 4 * g + 3
                    wait("tensor", sv, scan_done[(s, dc, n_hi)])
                    wait("tensor", ss, diag_done[(s, dc)])
                    for n in range(4 * g, 4 * g + 4):
                        run("tensor", lambda e, n=n, ab=dc % 2: e.matmul(
                            out=pm[:],
                            lhsT=diag_all[ab][:, n * 128:(n + 1) * 128],
                            rhs=s_all[:, (n % 12) * 512:(n % 12) * 512 + 512],
                            start=False, stop=False))
                        vt = run("tensor", lambda e, n=n, ab=dc % 2: e.matmul(
                            out=pop1[:, 1:512],
                            lhsT=diag_all[ab][:, (16 + n) * 128:(17 + n) * 128],
                            rhs=s_all[:, (n % 12) * 512:(n % 12) * 512 + 511],
                            start=False, stop=False), st_)
                        outproj_done[(s, dc, n)] = vt
                if eg is not None and k % 2 == 0:
                    j = k // 2
                    tb, ic = divmod(j, 4)
                    jb = j % 2
                    hk = (s, eg, j - 2) if j >= 2 else (s, eg - 1, 6 + j)
                    if hk in hmult_done:
                        wait("tensor", sv, hmult_done[hk])
                    vtg = None
                    for dcc in range(DC):
                        sem = st_ if dcc == 7 else None
                        vtg = run("tensor", lambda e, d=dcc, t=tb, i=ic, jb=jb,
                                  w=eg % 2, sl=slot: e.matmul(
                            out=pg[jb][:],
                            lhsT=wg_sb[w][:, d * 512 + i * 128:d * 512 + (i + 1) * 128],
                            rhs=xn_fm[sl][:, d * SEG + t * 512:d * SEG + (t + 1) * 512],
                            start=(d == 0), stop=(d == 7)), sem) or vtg
                    vtu = None
                    for dcc in range(DC):
                        sem = st_ if dcc == 7 else None
                        vtu = run("tensor", lambda e, d=dcc, t=tb, i=ic, jb=jb,
                                  w=eg % 2, sl=slot: e.matmul(
                            out=pu[jb][:],
                            lhsT=wu_sb[w][:, d * 512 + i * 128:d * 512 + (i + 1) * 128],
                            rhs=xn_fm[sl][:, d * SEG + t * 512:d * SEG + (t + 1) * 512],
                            start=(d == 0), stop=(d == 7)), sem) or vtu
                    pair_done[(s, eg, j)] = (vtg, vtu)
                if ed is not None and k % 2 == 1:
                    ttb = (k - 1) // 2
                    tb4 = ttb // 4
                    wait("tensor", sv, hmult_done[(s, ed, tb4 * 4 + 3)])
                    for dh in range(2):
                        yk = (s, ed, ttb - 1, dh) if ttb >= 1 else \
                             (s, ed - 1, TT - 1, dh)
                        if yk in yadd_done:
                            wait("tensor", sv, yadd_done[yk])
                        vtd = None
                        for ic2 in range(4):
                            sem = st_ if ic2 == 3 else None
                            vtd = run("tensor", lambda e, i=ic2, d=dh, t=ttb,
                                      w=ed % 2: e.matmul(
                                out=pw[:, d * 512:(d + 1) * 512],
                                lhsT=hm[w][:, i * SEG + t * 128:i * SEG + (t + 1) * 128],
                                rhs=wd_sb[w][:, i * 1024 + d * 512:i * 1024 + (d + 1) * 512],
                                start=(i == 0), stop=(i == 3)), sem) or vtd
                        down_done[(s, ed, ttb, dh)] = vtd
                if dc is not None:
                    n = k
                    col = dc * 16 + n
                    ki = scan_idx[0]
                    scan_idx[0] += 1
                    ok = (s, dc, n - 12) if n >= 12 else \
                         ((s, dc - 1, n + 4) if dc >= 1 else (s - 1, 7, n + 4))
                    if ok in outproj_done:
                        wait("vector", st_, outproj_done[ok])
                    if ki >= 12 and ki - 12 < len(copy_chain) and copy_chain[ki - 12] > 0:
                        wait("vector", ss, copy_chain[ki - 12])
                    xv = xn_fm[slot][:, dc * SEG:(dc + 1) * SEG].rearrange(
                        "p (m two) -> p m two", two=2)
                    run("vector", lambda e, c=col, xe=xv[:, :, 0:1],
                        xo=xv[:, :, 1:2], ub=u_buf[n % 2]: e.scalar_tensor_tensor(
                        out=ub[:].rearrange("p (m o) -> p m o", o=1),
                        in0=xe, scalar=q_all[:, c:c + 1], in1=xo,
                        op0=ALU.mult, op1=ALU.add))
                    init = 0.0 if s == 0 else states[:, col:col + 1]
                    sbap = s_all[:, (n % 12) * 512:(n % 12 + 1) * 512]
                    vsc = run("vector", lambda e, sb=sbap, c=col, init=init,
                              ub=u_buf[n % 2]: e.tensor_tensor_scan(
                        out=sb, data0=q2_all[:, c:c + 1].broadcast_to([128, 512]),
                        data1=ub[:], initial=init, op0=ALU.mult, op1=ALU.add), sv)
                    scan_done[(s, dc, n)] = vsc
                if dc is not None:
                    n = k
                    if s < NSEG - 1:
                        wait("scalar", sv, scan_done[(s, dc, n)])
                        sbap = s_all[:, (n % 12) * 512:(n % 12 + 1) * 512]
                        col = dc * 16 + n
                        cv = run("scalar", lambda e, sb=sbap, c=col: e.activation(
                            out=states[:, c:c + 1], in_=sb[:, 511:512],
                            func=AF.Copy), ss)
                        copy_chain.append(cv)
                    else:
                        copy_chain.append(0)

                if eg is not None and k % 2 == 1:
                    j = (k - 1) // 2
                    jb = j % 2
                    vtg, vtu = pair_done[(s, eg, j)]
                    wait("scalar", st_, vtg)
                    hk = (s, eg, j - 2) if j >= 2 else (s, eg - 1, 6 + j)
                    if hk in hmult_done:
                        wait("scalar", sv, hmult_done[hk])
                    vsl = run("scalar", lambda e, jb=jb: e.activation(
                        out=g_sb[jb][:], in_=pg[jb][:], func=AF.Silu), ss)
                    silu_done[(s, eg, j)] = vsl
                if dct is not None and k % 2 == 0 and ablate != "ssm":
                    jj = k // 2
                    if jj == 0:
                        wait("vector", sxt, sstm_done[(s, dct)])
                    sem = sv if jj == 7 else None
                    va = run("vector", lambda e, d=dct, t=jj: e.tensor_tensor(
                        out=y_acc[:, t * 1024 + d * 128:t * 1024 + (d + 1) * 128],
                        in0=ssm_tm[d % 2][:, t * 128:(t + 1) * 128],
                        in1=y_acc[:, t * 1024 + d * 128:t * 1024 + (d + 1) * 128],
                        op=ALU.add), sem)
                    if jj == 7:
                        sadd_done[(s, dct)] = va
                if ed is not None and k % 2 == 1:
                    ttb = (k - 1) // 2
                    for dh in range(2):
                        wait("vector", st_, down_done[(s, ed, ttb, dh)])
                        ysl = y_acc[:, ttb * 1024 + dh * 512:ttb * 1024 + (dh + 1) * 512]
                        if ablate == "ffn":
                            vya = run("vector", lambda e, ysl=ysl: e.tensor_scalar(
                                out=ysl, in0=ysl, scalar1=1.0, scalar2=None,
                                op0=ALU.mult), sv)
                            yadd_done[(s, ed, ttb, dh)] = vya
                            continue
                        if ed < 8:
                            vya = run("vector", lambda e, t=ttb, e_=ed, d=dh,
                                      ysl=ysl: e.scalar_tensor_tensor(
                                out=ysl, in0=pw[:, d * 512:(d + 1) * 512],
                                scalar=w_tm[:, t * 8 + e_:t * 8 + e_ + 1],
                                in1=ysl, op0=ALU.mult, op1=ALU.add), sv)
                        else:
                            vya = run("vector", lambda e, d=dh, ysl=ysl: e.tensor_tensor(
                                out=ysl, in0=pw[:, d * 512:(d + 1) * 512],
                                in1=ysl, op=ALU.add), sv)
                        yadd_done[(s, ed, ttb, dh)] = vya
                if eg is not None and k % 2 == 1:
                    j = (k - 1) // 2
                    jb = j % 2
                    vtg, vtu = pair_done[(s, eg, j)]
                    wait("vector", ss, silu_done[(s, eg, j)])
                    wait("vector", st_, vtu)
                    dk = (s, eg - 2, TT - 1, 1) if eg >= 2 else (s - 1, eg + 8, TT - 1, 1)
                    if j == 0 and dk in down_done:
                        wait("vector", st_, down_done[dk])
                    tb, ic = divmod(j, 4)
                    vh = run("vector", lambda e, jb=jb, t=tb, i=ic, w=eg % 2:
                             e.tensor_tensor(
                        out=hm[w][:, i * SEG + t * 512:i * SEG + (t + 1) * 512],
                        in0=pu[jb][:], in1=g_sb[jb][:], op=ALU.mult), sv)
                    hmult_done[(s, eg, j)] = vh
            if dc is not None:
                wait("tensor", sv, scan_done[(s, dc, 15)])
                wait("tensor", ss, diag_done[(s, dc)])
                for n in range(12, 16):
                    run("tensor", lambda e, n=n, ab=dc % 2: e.matmul(
                        out=pm[:],
                        lhsT=diag_all[ab][:, n * 128:(n + 1) * 128],
                        rhs=s_all[:, (n % 12) * 512:(n % 12) * 512 + 512],
                        start=False, stop=(n == 15)))
                    vt = run("tensor", lambda e, n=n, ab=dc % 2: e.matmul(
                        out=pop1[:, 1:512],
                        lhsT=diag_all[ab][:, (16 + n) * 128:(17 + n) * 128],
                        rhs=s_all[:, (n % 12) * 512:(n % 12) * 512 + 511],
                        start=False, stop=(n == 15)), st_)
                    outproj_done[(s, dc, n)] = vt
                wait("scalar", st_, vt)
                pk = (s, dc - 2) if dc >= 2 else (s - 1, dc + 6)
                if pk in sstm_done:
                    wait("scalar", sxt, sstm_done[pk])
                sv_o = sacc_bf[dc % 2][:].rearrange("p (m two) -> p m two", two=2)
                run("scalar", lambda e, o=sv_o[:, :, 1:2]: e.activation(
                    out=o, in_=pm[:].rearrange("p (m o) -> p m o", o=1),
                    func=AF.Copy))
                ev = run("scalar", lambda e, o=sv_o[:, :, 0:1]: e.activation(
                    out=o, in_=pop1[:].rearrange("p (m o) -> p m o", o=1),
                    func=AF.Copy), ss)
                evac_done[(s, dc)] = ev
                if s > 0:
                    wait("vector", ss, ev)
                    ev = run("vector", lambda e, ab=dc % 2, d=dc: e.tensor_tensor(
                        out=sacc_bf[ab][:, 0:1], in0=sacc_bf[ab][:, 0:1],
                        in1=corr8[:, d:d + 1], op=ALU.add), sv)
                if dc + 1 < 8:
                    run("scalar", lambda e, d=dc + 1: e.activation(
                        out=om_diag[d % 2][:], in_=id_f32[:], func=AF.Copy,
                        scale=om_t[:, d:d + 1]))
                    od = run("scalar", lambda e, d=dc + 1: e.activation(
                        out=ko_diag[d % 2][:], in_=id_f32[:], func=AF.Copy,
                        scale=kom[:, d:d + 1]), ss)
                    omdiag_done[(s, dc + 1)] = od
                    build_diags(dc + 1)
                if s > 0:
                    wait("sync", sv, ev)
                else:
                    wait("sync", ss, ev)
                pk2 = (s, dc - 2) if dc >= 2 else (s - 1, dc + 6)
                if pk2 in sadd_done:
                    wait("sync", sv, sadd_done[pk2])
                elif ablate == "ssm":
                    pass
                for tt2 in range(TT):
                    run("sync", lambda e, ab=dc % 2, t=tt2: e.dma_start_transpose(
                        out=ssm_tm[ab][:, t * 128:(t + 1) * 128],
                        in_=sacc_bf[ab][:, t * 128:(t + 1) * 128]), sxt, 16)
                sstm_done[(s, dc)] = sxt.v
            if eg is not None:
                tgt, te = next_exp(s, eg)
                if tgt is not None and tgt not in w_gu_ready:
                    wait("gpsimd", st_, st_.v)
                    w_gu_ready[tgt] = weight_batch_gu(te, eg % 2)
            if ed is not None:
                tgt, te = next_exp(s, ed)
                if tgt is not None and tgt not in w_d_ready:
                    wait("gpsimd", st_, st_.v)
                    w_d_ready[tgt] = weight_batch_d(te, ed % 2)
            if s + 1 < NSEG and 1 <= r <= 8:
                emit_p1_a(s + 1, r - 1)
            if s + 1 < NSEG and r == 9:
                emit_p2_logits(s + 1)
        seg_lastmm[s] = st_.v

    def emit_stores(s):
        for ttb in range(TT):
            wait("sync", sv, yadd_done[(s, NE - 1, ttb, 1)])
            rows = s * SEG + ttb * 128
            run("sync", lambda e, r=rows, t=ttb: e.dma_start(
                out=y_h[r:r + 128, :], in_=y_acc[:, t * 1024:(t + 1) * 1024]),
                sst, 16)
        store_done[s + 1] = sst.v


    for tt in range(4):
        emit_p1_load(0, tt)
    for tt in range(TT):
        emit_p1_tile(0, tt)
    emit_reloads(0)
    emit_p2_logits(0)
    emit_p2_rest(0)
    for s in range(NSEG):
        emit_rounds(s)
        emit_stores(s)
        if s + 1 < NSEG:
            emit_reloads(s + 1)
            emit_p2_rest(s + 1)

    if debug:
        wait("sync", sv, sv.v)
        wait("sync", ss, ss.v)
        wait("sync", st_, st_.v)
        dkom_h = nc.dram_tensor("dkom", [128, 8], F32, kind="ExternalOutput").ap()
        dcq_h = nc.dram_tensor("dcq", [128, 128], F32, kind="ExternalOutput").ap()
        dcorr_h = nc.dram_tensor("dcorr", [128, 8], F32, kind="ExternalOutput").ap()
        for dst_ap, src_t in [(dq_h, q_all), (dc_h, c_all), (dst_h, states),
                              (dkom_h, kom), (dcq_h, cq_all), (dcorr_h, corr8),
                              (dwtm_h, w_tm), (dsfc_h, sfc_tm),
                              (dxn_h, xn_fm[1]), (dssm_h, sacc_bf[1]),
                              (dxs_h, x_stage[1]), (dxnst_h, xn_stage[1]),
                              (dbn_h, bn_s)]:
            run("sync", lambda e, d=dst_ap, s=src_t: e.dma_start(out=d, in_=s[:]),
                sst, 16)
    wait("sync", sst, sst.v)

    with nc.Block() as block:
        @block.sync
        def _(e):
            for f in steps["sync"]:
                f(e)

        @block.gpsimd
        def _(e):
            for f in steps["gpsimd"]:
                f(e)

        @block.vector
        def _(e):
            for f in steps["vector"]:
                f(e)

        @block.scalar
        def _(e):
            for f in steps["scalar"]:
                f(e)

        @block.tensor
        def _(e):
            for f in steps["tensor"]:
                f(e)

    ctx.close()
    return nc


def kernel(**inputs):
    def np32(a):
        return np.ascontiguousarray(np.asarray(a, dtype=np.float32))

    x = np32(inputs["L0_attn_x"])
    delta = np32(inputs["L0_attn_delta"])[:, :, 0]
    alpha = np32(inputs["L0_attn_alpha"])[:, :, 0]
    beta = np32(inputs["L0_attn_beta"])[:, :, 0]
    gamma = np32(inputs["L0_attn_gamma"])
    omega = np32(inputs["L0_attn_omega"])
    gate_w = np32(inputs["L0_ffn_gate_w"])
    gate_bias = np32(inputs["L0_ffn_gate_bias"])
    w_gate = np32(inputs["L0_ffn_w_gate"])
    w_up = np32(inputs["L0_ffn_w_up"])
    w_down = np32(inputs["L0_ffn_w_down"])
    sw_gate = np32(inputs["L0_ffn_sw_gate"])
    sw_up = np32(inputs["L0_ffn_sw_up"])
    sw_down = np32(inputs["L0_ffn_sw_down"])
    _dbg = inputs.get("_debug", False)

    def chunk128(a2d, inner):
        return np.ascontiguousarray(
            a2d.reshape(8, 128, inner).transpose(1, 0, 2).reshape(128, 8 * inner))

    shared = {
        "delta_p": chunk128(delta, 16), "alpha_p": chunk128(alpha, 16),
        "beta_p": chunk128(beta, 16), "gamma_p": chunk128(gamma, 16),
        "omega_p": np.ascontiguousarray(omega.reshape(8, 128).T),
        "gw_p": chunk128(gate_w, 8),
        "gb_p": np.ascontiguousarray(gate_bias.reshape(8, 1)),
        "w_gate": w_gate, "w_up": w_up, "w_down": w_down,
        "sw_gate": sw_gate, "sw_up": sw_up, "sw_down": sw_down,
    }
    nc = build_nc(debug=_dbg, ablate=inputs.get('_ablate'))
    in_maps = [dict(shared, x=np.ascontiguousarray(x[b])) for b in range(B)]
    res = run_bass_kernel_spmd(nc, in_maps, core_ids=list(range(B)))
    if _dbg:
        return res
    out = np.stack([res.results[b]["y"] for b in range(B)], axis=0)
    return out.astype(np.float32)


if __name__ == "__main__":
    import reference as ref
    inp = ref.setup_inputs(0)
    want = np.asarray(ref.reference(**inp))
    got = kernel(**{k: np.asarray(v) for k, v in inp.items()})
    err = float(np.mean((got - want) ** 2) / np.mean(want ** 2))
    print("residual variance ratio:", err)
